# version 03
# baseline (speedup 1.0000x reference)
import contextlib
import numpy as np
import ml_dtypes
import concourse.bass as bass
import concourse.mybir as mybir
from concourse.bass_utils import run_bass_kernel_spmd

F32 = mybir.dt.float32
BF16 = mybir.dt.bfloat16
AF = mybir.ActivationFunctionType
ALU = mybir.AluOpType
AX = mybir.AxisListType

S = 4096
DM = 1024
DFF = 4096
NCS = 8
NFT = 24
NTC = 8
PI = float(np.pi)
TWO_PI = float(2.0 * np.pi)
MAGIC = 12582912.0
C1 = float(np.float32(TWO_PI))
C2 = float(np.float32(np.float64(TWO_PI) - np.float64(np.float32(TWO_PI))))
C3 = float(np.float64(TWO_PI) - np.float64(np.float32(TWO_PI))
           - np.float64(np.float32(np.float64(TWO_PI)
                                   - np.float64(np.float32(TWO_PI)))))
EPS_RMS = 1.1920929e-07
EPS_LN = 1e-5

ENGS = ("sync", "tensor", "scalar", "vector", "gpsimd")


class Tr:

    def __init__(self, nc):
        self.nc = nc
        self.q = {e: [] for e in ENGS}
        self.sems = {}
        self.cnt = {}
        for name in ("ld", "gd", "out", "te", "sc", "dv", "gp", "sy"):
            self.sems[name] = nc.alloc_semaphore(name)
            self.cnt[name] = 0

    def op(self, eng, fn, sem=None, k=1):
        if sem is None:
            self.q[eng].append(fn)
            return None
        self.cnt[sem] += k
        s = self.sems[sem]
        def run(e, fn=fn, s=s, k=k):
            fn(e).then_inc(s, k)
        self.q[eng].append(run)
        return self.cnt[sem]

    def dma(self, eng, fn, sem=None):
        if sem is None:
            sem = "ld" if eng == "sync" else "gd"
        return self.op(eng, fn, sem, 16)

    def wait(self, eng, sem, val):
        if val is None or val <= 0:
            return
        s = self.sems[sem]
        self.q[eng].append(lambda e, s=s, v=val: e.wait_ge(s, v))

    def now(self, sem):
        return self.cnt[sem]

    def barrier(self):
        self.wait("sync", "ld", self.cnt["ld"])
        self.wait("gpsimd", "gd", self.cnt["gd"])
        for eng in ENGS:
            self.cnt["sy"] += 1
            s = self.sems["sy"]
            self.q[eng].append(lambda e, s=s: e.sem_inc(s, 1))
        tgt = self.cnt["sy"]
        for eng in ENGS:
            self.wait(eng, "sy", tgt)


class Slot:

    def __init__(self):
        self.tok = None

    def acq(self, T, eng):
        if self.tok is not None:
            T.wait(eng, self.tok[0], self.tok[1])

    def rel(self, sem, cnt):
        self.tok = (sem, cnt)


def build():
    nc = bass.Bass("TRN2", target_bir_lowering=False, debug=False)

    xT = nc.dram_tensor("xT", [DM, S], F32, kind="ExternalInput").ap()
    wqkvT = nc.dram_tensor("wqkvT", [NFT * DM, 128], F32, kind="ExternalInput").ap()
    bqkv_c = nc.dram_tensor("bqkv_c", [128, NFT], F32, kind="ExternalInput").ap()
    woT = nc.dram_tensor("woT", [DM, DM], F32, kind="ExternalInput").ap()
    bo_c = nc.dram_tensor("bo_c", [128, 8], F32, kind="ExternalInput").ap()
    w1 = nc.dram_tensor("w1", [DM, DFF], F32, kind="ExternalInput").ap()
    b1_c = nc.dram_tensor("b1_c", [128, 32], F32, kind="ExternalInput").ap()
    th_c = nc.dram_tensor("th_c", [128, 32], F32, kind="ExternalInput").ap()
    wr_c = nc.dram_tensor("wr_c", [128, 32], F32, kind="ExternalInput").ap()
    wi_c = nc.dram_tensor("wi_c", [128, 32], F32, kind="ExternalInput").ap()
    w2 = nc.dram_tensor("w2", [DFF, DM], F32, kind="ExternalInput").ap()
    b2_c = nc.dram_tensor("b2_c", [128, 8], F32, kind="ExternalInput").ap()
    out_d = nc.dram_tensor("out", [DM, S], F32, kind="ExternalOutput").ap()

    qkvT_s = nc.dram_tensor("qkvT_s", [3 * DM, S], BF16, kind="Internal").ap()
    zbf_s = nc.dram_tensor("zbf_s", [DM, S], BF16, kind="Internal").ap()
    nm_s = nc.dram_tensor("nm_s", [1, 2], F32, kind="Internal").ap()
    w1_s = nc.dram_tensor("w1_s", [32 * 128, DM], BF16, kind="Internal").ap()
    w2_s = nc.dram_tensor("w2_s", [DFF, DM], BF16, kind="Internal").ap()

    eye_np = np.eye(128, dtype=ml_dtypes.bfloat16)
    eye_d = nc.inline_tensor(eye_np, "eye_bf").ap()
    am = np.zeros((128, 4, 512), dtype=np.float32)
    for off in range(4):
        for prt in range(128):
            am[prt, off, :off * 128 + prt] = -3.0e38
    mst_d = nc.inline_tensor(am.reshape(128, 2048), "amask_d").ap()

    T = Tr(nc)
    stack = contextlib.ExitStack()

    pA = stack.enter_context(nc.psum_tensor("pA", [128, 1024], F32))
    pB = stack.enter_context(nc.psum_tensor("pB", [128, 1024], F32))
    pC = stack.enter_context(nc.psum_tensor("pC", [128, 1024], F32))
    pD = stack.enter_context(nc.psum_tensor("pD", [128, 1024], F32))
    pD_bf = pD[:, :].bitcast(BF16)

    def mkpsl():
        d = {}
        for nm, (pt, off) in (("A0", (pA, 0)), ("A1", (pA, 512)),
                              ("B0", (pB, 0)), ("B1", (pB, 512)),
                              ("C0", (pC, 0)), ("C1", (pC, 512)),
                              ("D0", (pD, 0)), ("D1", (pD, 512))):
            d[nm] = (Slot(), pt, off)
        return d

    g = {}
    def sbg(name, shape, dtype):
        g[name] = stack.enter_context(nc.sbuf_tensor(name, shape, dtype))
        return g[name]

    sbg("eye", [128, 128], BF16)
    sbg("amask", [128, 2048], F32)
    sbg("epsr", [128, 1], F32)
    sbg("epsl", [128, 1], F32)
    sbg("ones_sq", [128, 128], BF16)
    sbg("hdiag", [128, 128], BF16)
    sbg("ones_row", [1, 128], BF16)
    sbg("bqkv", [128, NFT], F32)
    sbg("bo", [128, 8], F32)
    sbg("b1", [128, 32], F32)
    sbg("b2", [128, 8], F32)
    sbg("th", [128, 32], F32)
    sbg("wr", [128, 32], F32)
    sbg("wi", [128, 32], F32)
    sbg("kround", [128, 32], F32)
    sbg("b1s", [128, 32], F32)
    sbg("b1s2", [128, 32], F32)
    sbg("A1", [128, 32], F32)
    sbg("A2", [128, 32], F32)
    sbg("tmp32c", [128, 32], F32)
    sbg("thred", [128, 32], F32)
    sbg("sin_t", [128, 32], F32)
    sbg("cos_t", [128, 32], F32)
    sbg("ac", [128, 32], F32)
    sbg("tmp32", [128, 32], F32)
    sbg("tmp32b", [128, 32], F32)
    for i in range(8):
        sbg("wo_bf%d" % i, [128, 1024], BF16)
    wo_bf = [g["wo_bf%d" % i] for i in range(8)]

    for name, src in (("eye", eye_d), ("amask", mst_d), ("bqkv", bqkv_c),
                      ("bo", bo_c), ("b1", b1_c), ("th", th_c), ("wr", wr_c),
                      ("wi", wi_c), ("b2", b2_c)):
        t = g[name]
        T.dma("sync", lambda e, t=t, s=src: e.dma_start(out=t[:, :], in_=s[:, :]))
    ld_consts = T.now("ld")

    def vstep(fn):
        T.op("vector", fn, "dv")
        T.wait("vector", "dv", T.now("dv"))

    T.op("vector", lambda e: e.memset(g["epsr"][:, :], EPS_RMS))
    T.op("vector", lambda e: e.memset(g["epsl"][:, :], EPS_LN))
    T.op("vector", lambda e: e.memset(g["ones_sq"][:, :], 1.0))
    T.op("vector", lambda e: e.memset(g["ones_row"][:, :], 1.0))
    vstep(lambda e: e.memset(g["hdiag"][:, :], 0.0))
    T.op("vector", lambda e: e.memset(g["hdiag"][0:64, 0:64], 1.0))
    T.op("vector", lambda e: e.memset(g["hdiag"][64:128, 64:128], 1.0), "dv")
    dv_consts = T.now("dv")

    T.wait("vector", "ld", ld_consts)
    vstep(lambda e: e.tensor_scalar(
        g["kround"][:, :], g["th"][:, :], 1.0 / TWO_PI, MAGIC, ALU.mult, ALU.add))
    vstep(lambda e: e.tensor_scalar(
        g["kround"][:, :], g["kround"][:, :], MAGIC, None, ALU.subtract))
    vstep(lambda e: e.scalar_tensor_tensor(
        g["tmp32b"][:, :], g["kround"][:, :], -C1, g["th"][:, :],
        ALU.mult, ALU.add))
    vstep(lambda e: e.scalar_tensor_tensor(
        g["thred"][:, :], g["kround"][:, :], -C2, g["tmp32b"][:, :],
        ALU.mult, ALU.add))
    vstep(lambda e: e.tensor_scalar(
        g["b1s"][:, :], g["b1"][:, :], 0.1, None, ALU.mult))
    vstep(lambda e: e.tensor_scalar(
        g["b1s2"][:, :], g["b1s"][:, :], PI / 2.0, None, ALU.add))
    vstep(lambda e: e.tensor_scalar(
        g["tmp32"][:, :], g["thred"][:, :], PI / 2.0, None, ALU.add))
    vstep(lambda e: e.tensor_scalar(
        g["tmp32b"][:, :], g["tmp32"][:, :], -PI, None, ALU.is_lt))
    vstep(lambda e: e.tensor_scalar(
        g["tmp32c"][:, :], g["tmp32"][:, :], PI, None, ALU.is_gt))
    vstep(lambda e: e.tensor_sub(
        g["tmp32b"][:, :], g["tmp32b"][:, :], g["tmp32c"][:, :]))
    T.op("vector", lambda e: e.scalar_tensor_tensor(
        g["tmp32c"][:, :], g["tmp32b"][:, :], TWO_PI, g["tmp32"][:, :],
        ALU.mult, ALU.add), "dv")
    dv_theta = T.now("dv")
    T.wait("scalar", "dv", dv_theta)
    T.op("scalar", lambda e: e.activation(g["sin_t"][:, :], g["thred"][:, :], AF.Sin))
    T.op("scalar", lambda e: e.activation(g["cos_t"][:, :], g["tmp32c"][:, :], AF.Sin), "sc")
    sc_theta = T.now("sc")
    T.wait("vector", "sc", sc_theta)
    vstep(lambda e: e.tensor_mul(g["tmp32"][:, :], g["cos_t"][:, :], g["wr"][:, :]))
    vstep(lambda e: e.tensor_mul(g["tmp32b"][:, :], g["sin_t"][:, :], g["wi"][:, :]))
    vstep(lambda e: e.tensor_add(g["tmp32"][:, :], g["tmp32"][:, :], g["tmp32b"][:, :]))
    vstep(lambda e: e.tensor_mul(g["tmp32"][:, :], g["tmp32"][:, :], g["wi"][:, :]))
    vstep(lambda e: e.tensor_scalar(
        g["ac"][:, :], g["tmp32"][:, :], 0.1, None, ALU.mult))
    T.op("vector", lambda e: e.tensor_mul(g["A1"][:, :], g["ac"][:, :], g["sin_t"][:, :]))
    cA = T.op("vector", lambda e: e.tensor_mul(g["A2"][:, :], g["ac"][:, :], g["cos_t"][:, :]), "dv")

    with contextlib.ExitStack() as sa:
        a = {}
        def sba(name, shape, dtype):
            a[name] = sa.enter_context(nc.sbuf_tensor(name, shape, dtype))
            return a[name]

        for i in range(NFT):
            sba("wq%d" % i, [128, 1024], BF16)
        wq = [a["wq%d" % i] for i in range(NFT)]
        sba("wstg0", [128, 1024], F32)
        sba("wstg1", [128, 1024], F32)
        sba("pstg0", [128, 4096], F32)
        sba("pstg1", [128, 4096], F32)
        sba("pbf0", [128, 4096], BF16)
        sba("pbf1", [128, 4096], BF16)
        sba("xst0", [128, 4096], F32)
        sba("xst1", [128, 4096], F32)
        sba("x2b", [128, 4096], BF16)
        sba("stdt", [128, 512], F32)
        sba("rstd", [128, 512], F32)
        sba("nch0", [128, 4096], BF16)
        sba("nch1", [128, 4096], BF16)
        for i in range(4):
            sba("qstg%d" % i, [128, 512], BF16)
        qstg = [a["qstg%d" % i] for i in range(4)]

        wq_ready = []
        wstg_free = [Slot(), Slot()]
        for ft in range(NFT):
            sl = ft % 2
            wstg_free[sl].acq(T, "sync")
            wt = a["wstg%d" % sl]
            half = wqkvT[ft * DM:(ft + 1) * DM, :].rearrange(
                "(h cs p) f -> h p cs f", h=2, p=128)
            d0 = wt[:, 0:512].rearrange("p (cs f) -> p cs f", f=128)
            d1 = wt[:, 512:1024].rearrange("p (cs f) -> p cs f", f=128)
            T.dma("sync", lambda e, d=d0, s=half: e.dma_start(out=d, in_=s[0]))
            cld = T.dma("sync", lambda e, d=d1, s=half: e.dma_start(out=d, in_=s[1]))
            T.wait("gpsimd", "ld", cld)
            cc = T.op("gpsimd", lambda e, d=wq[ft], w=wt: e.tensor_copy(d[:, :], w[:, :]), "gp")
            wq_ready.append(cc)
            wstg_free[sl].rel("gp", cc)

        pro_slabs = ([("w1", cs) for cs in range(NCS)]
                     + [("w2", fs) for fs in range(32)])
        pstg_free = [Slot(), Slot()]
        pbf_free = [Slot(), Slot()]
        pro_i = 0

        def emit_prologue_slab():
            nonlocal pro_i
            if pro_i >= len(pro_slabs):
                return
            kind, idx = pro_slabs[pro_i]
            sl = pro_i % 2
            pro_i += 1
            pstg_free[sl].acq(T, "sync")
            st = a["pstg%d" % sl]
            if kind == "w1":
                cldp = T.dma("sync", lambda e, d=st,
                             s=w1[idx * 128:(idx + 1) * 128, :]:
                             e.dma_start(out=d[:, :], in_=s))
            else:
                cldp = T.dma("sync", lambda e, d=st,
                             s=w2[idx * 128:(idx + 1) * 128, :]:
                             e.dma_start(out=d[:, 0:1024], in_=s))
            T.wait("gpsimd", "ld", cldp)
            pbf_free[sl].acq(T, "gpsimd")
            bt = a["pbf%d" % sl]
            if kind == "w1":
                cc = T.op("gpsimd", lambda e, d=bt, s=st: e.tensor_scalar(
                    d[:, :], s[:, :], 0.1, None, ALU.mult), "gp")
                pstg_free[sl].rel("gp", cc)
                for ff in range(32):
                    T.dma("gpsimd", lambda e,
                          d=w1_s[ff * 128:(ff + 1) * 128, idx * 128:(idx + 1) * 128],
                          s=bt[:, ff * 128:(ff + 1) * 128]:
                          e.dma_start(out=d, in_=s))
            else:
                cc = T.op("gpsimd", lambda e, d=bt, s=st: e.tensor_copy(
                    d[:, 0:1024], s[:, 0:1024]), "gp")
                pstg_free[sl].rel("gp", cc)
                T.dma("gpsimd", lambda e, d=w2_s[idx * 128:(idx + 1) * 128, :],
                      s=bt[:, 0:1024]: e.dma_start(out=d, in_=s))
            pbf_free[sl].rel("gd", T.now("gd"))

        T.wait("scalar", "ld", ld_consts)
        xst_free = [Slot(), Slot()]
        nch_free = [Slot(), Slot()]
        qstg_free = [Slot() for _ in range(4)]
        ps_ev = [(pA, 0), (pA, 512), (pB, 0), (pB, 512)]
        ps_ev_free = [Slot() for _ in range(4)]
        ps_st_free = Slot()
        qidx = 0
        for tc in range(NTC):
            sl = tc % 2
            xst_free[sl].acq(T, "sync")
            xt = a["xst%d" % sl]
            xr = xT[:, tc * 512:(tc + 1) * 512].rearrange(
                "(h cs p) t -> h p cs t", h=2, p=128)
            xd0 = xt[:, 0:2048].rearrange("p (cs t) -> p cs t", t=512)
            xd1 = xt[:, 2048:4096].rearrange("p (cs t) -> p cs t", t=512)
            T.dma("sync", lambda e, d=xd0, s=xr: e.dma_start(out=d, in_=s[0]))
            cx = T.dma("sync", lambda e, d=xd1, s=xr: e.dma_start(out=d, in_=s[1]))
            T.wait("vector", "ld", cx)
            cdv = T.op("vector", lambda e, d=a["x2b"], s=xt: e.tensor_mul(
                d[:, :], s[:, :], s[:, :]), "dv")
            T.wait("tensor", "dv", cdv)
            ps_st_free.acq(T, "tensor")
            for cs in range(NCS):
                T.op("tensor", lambda e, r=a["x2b"][:, cs * 512:(cs + 1) * 512],
                     st=(cs == 0), sp=(cs == NCS - 1):
                     e.matmul(pC[:, 0:512], g["ones_sq"][:, :], r, start=st, stop=sp),
                     "te" if cs == NCS - 1 else None)
            cte = T.now("te")
            T.wait("scalar", "te", cte)
            csc = T.op("scalar", lambda e: e.activation(
                a["stdt"][:, :], pC[:, 0:512], AF.Sqrt,
                bias=g["epsr"][:, 0:1], scale=1.0 / DM), "sc")
            ps_st_free.rel("sc", csc)
            T.wait("vector", "sc", csc)
            T.op("vector", lambda e: e.reciprocal(
                a["rstd"][:, :], a["stdt"][:, :]))
            nch_free[sl].acq(T, "vector")
            nt = a["nch%d" % sl]
            cdn = None
            for cs in range(NCS):
                cdn = T.op("vector", lambda e, d=nt[:, cs * 512:(cs + 1) * 512],
                           s=xt[:, cs * 512:(cs + 1) * 512]:
                           e.tensor_mul(d, s, a["rstd"][:, :]), "dv")
            xst_free[sl].rel("dv", cdn)
            for ft in range(NFT):
                T.wait("tensor", "gp", wq_ready[ft])
                T.wait("tensor", "dv", cdn)
                pi = qidx % 4
                qidx += 1
                pt, off = ps_ev[pi]
                ps_ev_free[pi].acq(T, "tensor")
                for cs in range(NCS):
                    T.op("tensor", lambda e, w=wq[ft][:, cs * 128:(cs + 1) * 128],
                         r=nt[:, cs * 512:(cs + 1) * 512], p=pt[:, off:off + 512],
                         st=(cs == 0), sp=(cs == NCS - 1):
                         e.matmul(p, w, r, start=st, stop=sp),
                         "te" if cs == NCS - 1 else None)
                cmt = T.now("te")
                qstg_free[pi].acq(T, "scalar")
                T.wait("scalar", "te", cmt)
                cev = T.op("scalar", lambda e, d=qstg[pi], p=pt[:, off:off + 512],
                           b=g["bqkv"][:, ft:ft + 1]:
                           e.activation(d[:, :], p, AF.Identity, bias=b), "sc")
                ps_ev_free[pi].rel("sc", cev)
                T.wait("gpsimd", "sc", cev)
                cst = T.dma("gpsimd", lambda e,
                            d=qkvT_s[ft * 128:(ft + 1) * 128, tc * 512:(tc + 1) * 512],
                            s=qstg[pi]: e.dma_start(out=d, in_=s[:, :]))
                qstg_free[pi].rel("gd", cst)
                if ft % 5 == 4:
                    emit_prologue_slab()
            nch_free[sl].rel("te", T.now("te"))
        while pro_i < len(pro_slabs):
            emit_prologue_slab()

        T.barrier()

    with contextlib.ExitStack() as sbx:
        b = {}
        def sbb(name, shape, dtype):
            b[name] = sbx.enter_context(nc.sbuf_tensor(name, shape, dtype))
            return b[name]

        sbb("qt", [128, 4096], BF16)
        sbb("kt", [128, 4096], BF16)
        sbb("vt", [128, 4096], BF16)
        E = [[sbb("E%d_%d" % (h, k), [128, 4096], BF16) for k in range(4)]
             for h in range(2)]
        sbb("sqb", [128, 4096], BF16)
        sbb("zsb", [128, 4096], F32)
        sbb("zbfp", [128, 4096], BF16)
        sbb("qkacc", [128, 512], F32)
        sbb("negmx", [128, 1], F32)
        sbb("negmb", [128, 1], BF16)
        sbb("tmp11", [1, 1], BF16)
        sbb("negm0", [128, 1], F32)
        sbb("negm1", [128, 1], F32)
        vtr = [sbb("vtr%d" % k, [128, 128], BF16) for k in range(4)]
        wti = [sbb("wti%d" % k, [128, 128], BF16) for k in range(4)]
        sbb("spart", [128, 32], F32)
        sbb("ssum", [128, 2], F32)
        sbb("rs", [128, 2], F32)
        sbb("wostg0", [128, 1024], F32)
        sbb("wostg1", [128, 1024], F32)

        wostg_free = [Slot(), Slot()]
        for zf in range(8):
            sl = zf % 2
            wostg_free[sl].acq(T, "sync")
            wt = b["wostg%d" % sl]
            cldw = T.dma("sync", lambda e, d=wt, s=woT[zf * 128:(zf + 1) * 128, :]:
                         e.dma_start(out=d[:, :], in_=s))
            T.wait("gpsimd", "ld", cldw)
            cc = T.op("gpsimd", lambda e, d=wo_bf[zf], s=wt:
                      e.tensor_copy(d[:, :], s[:, :]), "gp")
            wostg_free[sl].rel("gp", cc)

        psl = mkpsl()
        qkv_free = Slot()
        E_free = [[Slot() for _ in range(4)] for _ in range(2)]
        wti_free = [Slot() for _ in range(4)]
        spart_free = Slot()
        zsb_free = Slot()
        zbfp_free = Slot()
        sqb_free = Slot()

        for p in range(8):
            qkv_free.acq(T, "sync")
            T.dma("sync", lambda e, p=p, d=b["qt"]: e.dma_start(
                out=d[:, :], in_=qkvT_s[128 * p:128 * p + 128, :]))
            T.dma("sync", lambda e, p=p, d=b["kt"]: e.dma_start(
                out=d[:, :], in_=qkvT_s[DM + 128 * p:DM + 128 * p + 128, :]))
            cldv = T.dma("sync", lambda e, p=p, d=b["vt"]: e.dma_start(
                out=d[:, :], in_=qkvT_s[2 * DM + 128 * p:2 * DM + 128 * p + 128, :]))

            T.wait("vector", "ld", cldv)
            zsb_free.acq(T, "vector")
            for qi in range(2):
                src = b["qt"] if qi == 0 else b["kt"]
                sqb_free.acq(T, "vector")
                cdq = T.op("vector", lambda e, s=src: e.tensor_mul(
                    b["sqb"][:, :], s[:, :], s[:, :]), "dv")
                for ch in range(8):
                    nm = "C0" if ch % 2 == 0 else "C1"
                    slot, pt, off = psl[nm]
                    slot.acq(T, "tensor")
                    T.wait("tensor", "dv", cdq)
                    cmn = T.op("tensor", lambda e, r=b["sqb"][:, ch * 512:(ch + 1) * 512],
                               pp=pt[:, off:off + 512]:
                               e.matmul(pp, g["hdiag"][:, :], r, start=True, stop=True),
                               "te")
                    T.wait("vector", "te", cmn)
                    if qi == 0 and ch == 0:
                        cdm = T.op("vector", lambda e, pp=pt[:, off:off + 512]:
                                   e.tensor_copy(b["qkacc"][:, :], pp), "dv")
                    else:
                        cdm = T.op("vector", lambda e, pp=pt[:, off:off + 512]:
                                   e.tensor_max(b["qkacc"][:, :], b["qkacc"][:, :], pp),
                                   "dv")
                    slot.rel("dv", cdm)
                sqb_free.rel("te", T.now("te"))
            T.wait("vector", "dv", T.now("dv"))
            cdn = T.op("vector", lambda e: e.tensor_reduce(
                b["negmx"][:, :], b["qkacc"][:, :], AX.X, ALU.max, negate=True),
                "dv")
            T.wait("gpsimd", "dv", cdn)
            T.dma("gpsimd", lambda e: e.dma_start(
                out=nm_s[0:1, 0:1], in_=b["negmx"][0:1, 0:1]))
            cg1 = T.dma("gpsimd", lambda e: e.dma_start(
                out=nm_s[0:1, 1:2], in_=b["negmx"][64:65, 0:1]))
            T.wait("gpsimd", "gd", cg1)
            T.dma("gpsimd", lambda e: e.dma_start(
                out=b["negm0"][:, :], in_=nm_s[0:1, 0:1].broadcast_to([128, 1])))
            cg2 = T.dma("gpsimd", lambda e: e.dma_start(
                out=b["negm1"][:, :], in_=nm_s[0:1, 1:2].broadcast_to([128, 1])))
            T.wait("gpsimd", "gd", cg2)
            T.wait("scalar", "gd", cg2)
            negm = [b["negm0"], b["negm1"]]
            for gi in range(8):
                L = 8 - gi
                glo = 512 * gi
                psl["D0"][0].acq(T, "tensor")
                T.wait("tensor", "ld", cldv)
                for k in range(4):
                    s = 4 * gi + k
                    T.op("tensor", lambda e, s=s, k=k:
                         e.transpose(pD_bf[:, 128 * k:128 * k + 128],
                                     b["vt"][:, 128 * s:128 * s + 128],
                                     g["eye"][:, :]), "te")
                ctv = T.now("te")
                T.wait("vector", "te", ctv)
                cve = None
                for k in range(4):
                    cve = T.op("vector", lambda e, k=k: e.tensor_copy(
                        vtr[k][:, :], pD_bf[:, 128 * k:128 * k + 128]), "dv")
                psl["D0"][0].rel("dv", cve)

                slot_pairs = (("A0", "A1", "B0", "B1"), ("C0", "C1", "D0", "D1"))
                spart_free.acq(T, "scalar")
                for k in range(4):
                    s = 4 * gi + k
                    npair = (L + 1) // 2
                    for jp in range(npair):
                        j0 = 2 * jp
                        wide = (j0 + 1 < L)
                        names = slot_pairs[jp % 2]
                        for h in range(2):
                            t0 = psl[names[2 * h]]
                            t1 = psl[names[2 * h + 1]]
                            t0[0].acq(T, "tensor")
                            if wide:
                                t1[0].acq(T, "tensor")
                            for jj in range(2 if wide else 1):
                                tt = t0 if jj == 0 else t1
                                qcol = glo + (j0 + jj) * 512
                                T.op("tensor", lambda e, h=h, s=s, qcol=qcol,
                                     pp=tt[1][:, tt[2]:tt[2] + 512]:
                                     e.matmul(pp,
                                              b["kt"][64 * h:64 * h + 64,
                                                      128 * s:128 * s + 128],
                                              b["qt"][64 * h:64 * h + 64,
                                                      qcol:qcol + 512],
                                              start=True, stop=True), "te")
                        cms = T.now("te")
                        if jp == 0:
                            T.wait("vector", "te", cms)
                            for h in range(2):
                                tt = psl[names[2 * h]]
                                T.op("vector", lambda e, s=s,
                                     pp=tt[1][:, tt[2]:tt[2] + 512]:
                                     e.tensor_add(
                                         pp, pp,
                                         g["amask"][:, (s % 4) * 512:
                                                    (s % 4) * 512 + 512]), "dv")
                            cmask = T.now("dv")
                            T.wait("scalar", "dv", cmask)
                        T.wait("scalar", "te", cms)
                        for h in range(2):
                            t0 = psl[names[2 * h]]
                            t1 = psl[names[2 * h + 1]]
                            fd = 1024 if wide else 512
                            if jp == 0:
                                E_free[h][k].acq(T, "scalar")
                            acc = b["spart"][:, h * 16 + k * 4 + jp:
                                             h * 16 + k * 4 + jp + 1]
                            cse = T.op("scalar", lambda e, h=h, k=k, j0=j0, fd=fd,
                                       src=t0[1][:, t0[2]:t0[2] + fd],
                                       ac=acc:
                                       e.activation(E[h][k][:, j0 * 512:j0 * 512 + fd],
                                                    src, AF.Exp,
                                                    bias=negm[h][:, 0:1],
                                                    accum_out=ac), "sc")
                            t0[0].rel("sc", cse)
                            if wide:
                                t1[0].rel("sc", cse)
                    cstrip = T.now("sc")
                    T.wait("vector", "sc", cstrip)
                    wti_free[k].acq(T, "vector")
                    cwt = None
                    for h in range(2):
                        T.op("vector", lambda e, h=h, k=k, n=npair:
                             e.reduce_sum(b["ssum"][:, h:h + 1],
                                          b["spart"][:, h * 16 + k * 4:
                                                     h * 16 + k * 4 + n], AX.X),
                             "dv" if h == 1 else None)
                    T.wait("vector", "dv", T.now("dv"))
                    for h in range(2):
                        T.op("vector", lambda e, h=h: e.reciprocal(
                            b["rs"][:, h:h + 1], b["ssum"][:, h:h + 1]))
                    for h in range(2):
                        cwt = T.op("vector", lambda e, h=h, k=k: e.tensor_scalar(
                            wti[k][:, 64 * h:64 * h + 64],
                            vtr[k][:, 64 * h:64 * h + 64],
                            b["rs"][:, h:h + 1], None, ALU.mult), "dv")
                spart_free.rel("dv", T.now("dv"))
                cwall = T.now("dv")
                zrot = ("A0", "A1", "B0", "B1")
                for j in range(L):
                    zt = psl[zrot[j % 4]]
                    zt[0].acq(T, "tensor")
                    T.wait("tensor", "dv", cwall)
                    for k in range(4):
                        for h in range(2):
                            T.op("tensor", lambda e, k=k, h=h, j=j,
                                 pp=zt[1][64 * h:64 * h + 64, zt[2]:zt[2] + 512]:
                                 e.matmul(pp, wti[k][:, 64 * h:64 * h + 64],
                                          E[h][k][:, j * 512:(j + 1) * 512],
                                          start=(k == 0),
                                          stop=(k == 3),
                                          skip_group_check=True,
                                          tile_position=(0, 64 * h)),
                                 "te" if (k == 3 and h == 1) else None)
                    cmz = T.now("te")
                    T.wait("vector", "te", cmz)
                    qq = 512 * (gi + j)
                    if gi == 0:
                        cvz = T.op("vector", lambda e, qq=qq,
                                   pp=zt[1][:, zt[2]:zt[2] + 512]:
                                   e.tensor_copy(b["zsb"][:, qq:qq + 512], pp), "dv")
                    else:
                        cvz = T.op("vector", lambda e, qq=qq,
                                   pp=zt[1][:, zt[2]:zt[2] + 512]:
                                   e.tensor_add(b["zsb"][:, qq:qq + 512],
                                                b["zsb"][:, qq:qq + 512], pp), "dv")
                    zt[0].rel("dv", cvz)
                czg = T.now("te")
                for k in range(4):
                    for h in range(2):
                        E_free[h][k].rel("te", czg)
                    wti_free[k].rel("te", czg)
            czd = T.now("dv")
            T.wait("gpsimd", "dv", czd)
            zbfp_free.acq(T, "gpsimd")
            ccz = T.op("gpsimd", lambda e: e.tensor_copy(
                b["zbfp"][:, :], b["zsb"][:, :]), "gp")
            zsb_free.rel("gp", ccz)
            cstz = T.dma("gpsimd", lambda e, p=p: e.dma_start(
                out=zbf_s[128 * p:128 * p + 128, :], in_=b["zbfp"][:, :]))
            zbfp_free.rel("gd", cstz)
            qkv_free.rel("te", T.now("te"))

        T.barrier()

    with contextlib.ExitStack() as sc_:
        c = {}
        def sbc(name, shape, dtype):
            c[name] = sc_.enter_context(nc.sbuf_tensor(name, shape, dtype))
            return c[name]

        zbt = [[sbc("zb%d_%d" % (sl, i), [128, 512], BF16) for i in range(8)]
               for sl in range(2)]
        xtt = [[sbc("xt%d_%d" % (sl, i), [128, 512], F32) for i in range(8)]
               for sl in range(2)]
        s1 = [sbc("s1_%d" % i, [128, 512], F32) for i in range(8)]
        x1 = [sbc("x1_%d" % i, [128, 512], F32) for i in range(8)]
        s1b = [sbc("s1b_%d" % i, [128, 512], BF16) for i in range(8)]
        sq = [sbc("sq_%d" % i, [128, 512], BF16) for i in range(8)]
        x1b = [sbc("x1b_%d" % i, [128, 512], BF16) for i in range(8)]
        actb = [sbc("act_%d" % i, [128, 512], BF16) for i in range(32)]
        w1sl = [sbc("w1sl%d" % i, [128, 1024], BF16) for i in range(2)]
        w2r = [sbc("w2r%d" % i, [128, 256], BF16) for i in range(32)]
        hh = [sbc("hh_%d" % i, [128, 512], F32) for i in range(2)]
        aw = [sbc("aw_%d" % i, [128, 512], F32) for i in range(2)]
        sn = [sbc("sn_%d" % i, [128, 512], F32) for i in range(2)]
        ot = [sbc("ot_%d" % i, [128, 512], F32) for i in range(2)]
        sbc("t1", [128, 512], F32)
        sbc("t2", [128, 512], F32)
        sbc("rstdc", [128, 512], F32)

        psl = mkpsl()
        io_free = [Slot(), Slot()]
        s1_free = Slot()
        x1_free = Slot()
        x1b_free = Slot()
        act_free = Slot()
        w1sl_free = [Slot(), Slot()]
        w2sl_free = [Slot(), Slot()]
        w2res_free = Slot()
        ffs_free = [Slot(), Slot()]
        ot_free = [Slot(), Slot()]
        stat_free = Slot()

        def ln_stats(bsrc, sqsrc, lnc):
            psl["D0"][0].acq(T, "tensor")
            psl["D1"][0].acq(T, "tensor")
            T.wait("tensor", "dv", lnc)
            for cs in range(8):
                T.op("tensor", lambda e, cs=cs, r=bsrc[cs]: e.matmul(
                    pD[:, 0:512], g["ones_sq"][:, :], r[:, :],
                    start=(cs == 0), stop=(cs == 7)),
                    "te" if cs == 7 else None)
            for cs in range(8):
                T.op("tensor", lambda e, cs=cs, r=sqsrc[cs]: e.matmul(
                    pD[:, 512:1024], g["ones_sq"][:, :], r[:, :],
                    start=(cs == 0), stop=(cs == 7)),
                    "te" if cs == 7 else None)
            cst = T.now("te")
            stat_free.acq(T, "scalar")
            T.wait("scalar", "te", cst)
            csq = T.op("scalar", lambda e: e.activation(
                c["t1"][:, :], pD[:, 0:512], AF.Square, scale=1.0 / DM), "sc")
            T.wait("vector", "sc", csq)
            T.op("vector", lambda e: e.scalar_tensor_tensor(
                c["t2"][:, :], pD[:, 512:1024], 1.0 / DM, c["t1"][:, :],
                ALU.mult, ALU.subtract), "dv")
            cvar = T.now("dv")
            T.wait("scalar", "dv", cvar)
            T.op("scalar", lambda e: e.activation(
                c["t1"][:, :], c["t2"][:, :], AF.Sqrt,
                bias=g["epsl"][:, 0:1]), "sc")
            csd = T.now("sc")
            T.wait("vector", "sc", csd)
            crst = T.op("vector", lambda e: e.reciprocal(
                c["rstdc"][:, :], c["t1"][:, :]), "dv")
            psl["D1"][0].rel("dv", cvar)
            return crst

        for tc in range(NTC):
            sl = tc % 2
            io_free[sl].acq(T, "sync")
            for zf in range(8):
                T.dma("sync", lambda e, tc=tc, zf=zf, d=zbt[sl][zf]: e.dma_start(
                    out=d[:, :],
                    in_=zbf_s[zf * 128:(zf + 1) * 128, tc * 512:(tc + 1) * 512]))
            cio = None
            for cs in range(8):
                cio = T.dma("sync", lambda e, tc=tc, cs=cs, d=xtt[sl][cs]:
                            e.dma_start(out=d[:, :],
                                        in_=xT[cs * 128:(cs + 1) * 128,
                                               tc * 512:(tc + 1) * 512]))
            s1_free.acq(T, "vector")
            cs1 = None
            for fo in range(8):
                wslot = psl["C0" if fo % 2 == 0 else "C1"]
                wslot[0].acq(T, "tensor")
                T.wait("tensor", "ld", cio)
                for zf in range(8):
                    T.op("tensor", lambda e, zf=zf, fo=fo, sl=sl,
                         pp=wslot[1][:, wslot[2]:wslot[2] + 512]:
                         e.matmul(pp, wo_bf[zf][:, fo * 128:(fo + 1) * 128],
                                  zbt[sl][zf][:, :], start=(zf == 0), stop=(zf == 7)),
                         "te" if zf == 7 else None)
                cwo = T.now("te")
                T.wait("vector", "te", cwo)
                cs1 = T.op("vector", lambda e, fo=fo, sl=sl,
                           pp=wslot[1][:, wslot[2]:wslot[2] + 512]:
                           e.scalar_tensor_tensor(
                               s1[fo][:, :], pp, g["bo"][:, fo:fo + 1],
                               xtt[sl][fo][:, :], ALU.add, ALU.add), "dv")
                wslot[0].rel("dv", cs1)
            cdsq = None
            for fo in range(8):
                cdsq = T.op("vector", lambda e, fo=fo: e.tensor_mul(
                    sq[fo][:, :], s1[fo][:, :], s1[fo][:, :]),
                    "dv" if fo == 7 else None)
            for fo in range(8):
                T.op("vector", lambda e, fo=fo: e.tensor_copy(
                    s1b[fo][:, :], s1[fo][:, :]), "dv" if fo == 7 else None)
            cgb1 = T.now("dv")
            T.wait("tensor", "dv", cgb1)
            crst = ln_stats(s1b, sq, cgb1)
            x1_free.acq(T, "vector")
            x1b_free.acq(T, "vector")
            c_x1 = None
            for fo in range(8):
                T.op("vector", lambda e, fo=fo: e.scalar_tensor_tensor(
                    c["t2"][:, :], pD[:, 0:512], -1.0 / DM, s1[fo][:, :],
                    ALU.mult, ALU.add))
                c_x1 = T.op("vector", lambda e, fo=fo: e.tensor_mul(
                    x1[fo][:, :], c["t2"][:, :], c["rstdc"][:, :]), "dv")
                T.op("vector", lambda e, fo=fo: e.tensor_copy(
                    x1b[fo][:, :], x1[fo][:, :]), "dv")
            psl["D0"][0].rel("dv", c_x1)
            cx1b = T.now("dv")
            s1_free.rel("dv", cx1b)
            act_free.acq(T, "vector")
            cact = None
            for ff in range(32):
                wsl_i = ff % 2
                w1sl_free[wsl_i].acq(T, "sync")
                cw1 = T.dma("sync", lambda e, ff=ff, d=w1sl[wsl_i]: e.dma_start(
                    out=d[:, :], in_=w1_s[ff * 128:(ff + 1) * 128, :]))
                hslot = psl["A0" if ff % 2 == 0 else "A1"]
                hslot[0].acq(T, "tensor")
                T.wait("tensor", "ld", cw1)
                T.wait("tensor", "dv", cx1b)
                for cs in range(8):
                    T.op("tensor", lambda e, cs=cs, w=w1sl[wsl_i],
                         pp=hslot[1][:, hslot[2]:hslot[2] + 512]:
                         e.matmul(pp, w[:, cs * 128:(cs + 1) * 128],
                                  x1b[cs][:, :], start=(cs == 0), stop=(cs == 7)),
                         "te" if cs == 7 else None)
                ch = T.now("te")
                w1sl_free[wsl_i].rel("te", ch)
                fsl = ff % 2
                ffs_free[fsl].acq(T, "scalar")
                T.wait("scalar", "te", ch)
                T.op("scalar", lambda e, ff=ff, fsl=fsl,
                     pp=hslot[1][:, hslot[2]:hslot[2] + 512]:
                     e.activation(aw[fsl][:, :], pp, AF.Sin,
                                  bias=g["b1s"][:, ff:ff + 1]))
                csc2 = T.op("scalar", lambda e, ff=ff, fsl=fsl,
                            pp=hslot[1][:, hslot[2]:hslot[2] + 512]:
                            e.activation(sn[fsl][:, :], pp, AF.Sin,
                                         bias=g["b1s2"][:, ff:ff + 1]), "sc")
                T.wait("vector", "sc", csc2)
                chv = T.op("vector", lambda e, ff=ff, fsl=fsl,
                           pp=hslot[1][:, hslot[2]:hslot[2] + 512]:
                           e.tensor_scalar(hh[fsl][:, :], pp, 10.0,
                                           g["b1"][:, ff:ff + 1],
                                           ALU.mult, ALU.add), "dv")
                hslot[0].rel("dv", chv)
                T.op("vector", lambda e, ff=ff, fsl=fsl: e.scalar_tensor_tensor(
                    aw[fsl][:, :], aw[fsl][:, :], g["A2"][:, ff:ff + 1],
                    hh[fsl][:, :], ALU.mult, ALU.add))
                T.op("vector", lambda e, ff=ff, fsl=fsl: e.scalar_tensor_tensor(
                    hh[fsl][:, :], sn[fsl][:, :], g["A1"][:, ff:ff + 1],
                    aw[fsl][:, :], ALU.mult, ALU.add))
                cact = T.op("vector", lambda e, ff=ff, fsl=fsl: e.tensor_scalar(
                    actb[ff][:, :], hh[fsl][:, :], 0.0, None, ALU.max), "dv")
                ffs_free[fsl].rel("dv", cact)
            cacts = cact
            x1b_free.rel("te", T.now("te"))
            s1_free.acq(T, "vector")
            cs2all = None
            for sw in range(4):
                p0, p1 = psl["B0"], psl["B1"]
                p0[0].acq(T, "tensor")
                p1[0].acq(T, "tensor")
                T.wait("tensor", "dv", cacts)
                w2res_free.acq(T, "sync")
                cw2 = None
                for ff in range(32):
                    cw2 = T.dma("sync", lambda e, ff=ff, sw=sw, d=w2r[ff]:
                                e.dma_start(out=d[:, :],
                                            in_=w2_s[ff * 128:(ff + 1) * 128,
                                                     sw * 256:(sw + 1) * 256]))
                T.wait("tensor", "ld", cw2)
                for ff in range(32):
                    for fq in range(2):
                        T.op("tensor", lambda e, ff=ff, fq=fq,
                             w=w2r[ff][:, fq * 128:(fq + 1) * 128],
                             pp=(p0 if fq == 0 else p1):
                             e.matmul(pp[1][:, pp[2]:pp[2] + 512], w,
                                      actb[ff][:, :], start=(ff == 0),
                                      stop=(ff == 31), skip_group_check=True),
                             "te" if fq == 1 else None)
                cw2t = T.now("te")
                w2res_free.rel("te", cw2t)
                T.wait("vector", "te", cw2t)
                for fq in range(2):
                    fo = sw * 2 + fq
                    pp = (p0 if fq == 0 else p1)
                    cs2all = T.op("vector", lambda e, fo=fo, pp=pp:
                                  e.scalar_tensor_tensor(
                                      s1[fo][:, :], pp[1][:, pp[2]:pp[2] + 512],
                                      g["b2"][:, fo:fo + 1], x1[fo][:, :],
                                      ALU.add, ALU.add), "dv")
                    pp[0].rel("dv", cs2all)
            act_free.rel("te", T.now("te"))
            x1_free.rel("dv", cs2all)
            cdsq2 = None
            for fo in range(8):
                cdsq2 = T.op("vector", lambda e, fo=fo: e.tensor_mul(
                    sq[fo][:, :], s1[fo][:, :], s1[fo][:, :]),
                    "dv" if fo == 7 else None)
            for fo in range(8):
                T.op("vector", lambda e, fo=fo: e.tensor_copy(
                    s1b[fo][:, :], s1[fo][:, :]), "dv" if fo == 7 else None)
            cgb2 = T.now("dv")
            T.wait("tensor", "dv", cgb2)
            crst2 = ln_stats(s1b, sq, cgb2)
            lastap = None
            for fo in range(8):
                osl = fo % 2
                ot_free[osl].acq(T, "vector")
                T.op("vector", lambda e, fo=fo: e.scalar_tensor_tensor(
                    c["t2"][:, :], pD[:, 0:512], -1.0 / DM, s1[fo][:, :],
                    ALU.mult, ALU.add))
                lastap = T.op("vector", lambda e, fo=fo, d=ot[osl]: e.tensor_mul(
                    d[:, :], c["t2"][:, :], c["rstdc"][:, :]), "dv")
                T.wait("gpsimd", "dv", lastap)
                cso = T.dma("gpsimd", lambda e, fo=fo, tc=tc, s=ot[osl]:
                            e.dma_start(out=out_d[fo * 128:(fo + 1) * 128,
                                                  tc * 512:(tc + 1) * 512],
                                        in_=s[:, :]), sem="out")
                ot_free[osl].rel("out", cso)
            psl["D0"][0].rel("dv", lastap)
            s1_free.rel("dv", cgb2)
            stat_free.rel("dv", lastap)
            io_free[sl].rel("dv", lastap)

        T.wait("sync", "out", T.now("out"))

    with nc.Block() as block:
        @block.sync
        def _(e):
            for fn in T.q["sync"]:
                fn(e)

        @block.tensor
        def _(e):
            for fn in T.q["tensor"]:
                fn(e)

        @block.scalar
        def _(e):
            for fn in T.q["scalar"]:
                fn(e)

        @block.vector
        def _(e):
            for fn in T.q["vector"]:
                fn(e)

        @block.gpsimd
        def _(e):
            for fn in T.q["gpsimd"]:
                fn(e)

    stack.close()
    return nc


_PERM = np.array([h * 192 + d * 3 + s for s in range(3)
                  for h in range(16) for d in range(64)])


def _host_layouts(inputs):
    f32 = np.float32
    Wqkv = np.asarray(inputs["L0_attn_W_qkv"], dtype=f32)
    bqkv = np.asarray(inputs["L0_attn_b_qkv"], dtype=f32)
    Wo = np.asarray(inputs["L0_attn_W_o"], dtype=f32)
    bo = np.asarray(inputs["L0_attn_b_o"], dtype=f32)
    W1 = np.asarray(inputs["L0_ffn_W1"], dtype=f32)
    b1 = np.asarray(inputs["L0_ffn_b1"], dtype=f32)
    th = np.asarray(inputs["L0_ffn_quantum_phases"], dtype=f32)
    wr = np.asarray(inputs["L0_ffn_quantum_weights_real"], dtype=f32)
    wi = np.asarray(inputs["L0_ffn_quantum_weights_imag"], dtype=f32)
    W2 = np.asarray(inputs["L0_ffn_W2"], dtype=f32)
    b2 = np.asarray(inputs["L0_ffn_b2"], dtype=f32)

    WqkvT_r = Wqkv[_PERM].T
    blocks = np.ascontiguousarray(
        WqkvT_r.reshape(DM, NFT, 128).transpose(1, 0, 2)).reshape(NFT * DM, 128)
    return {
        "wqkvT": np.ascontiguousarray(blocks),
        "bqkv_c": np.ascontiguousarray(bqkv[_PERM].reshape(NFT, 128).T),
        "woT": np.ascontiguousarray(Wo.T),
        "bo_c": np.ascontiguousarray(bo.reshape(8, 128).T),
        "w1": np.ascontiguousarray(W1),
        "b1_c": np.ascontiguousarray(b1.reshape(32, 128).T),
        "th_c": np.ascontiguousarray(th.reshape(32, 128).T),
        "wr_c": np.ascontiguousarray(wr.reshape(32, 128).T),
        "wi_c": np.ascontiguousarray(wi.reshape(32, 128).T),
        "w2": np.ascontiguousarray(W2),
        "b2_c": np.ascontiguousarray(b2.reshape(8, 128).T),
    }


def kernel(**inputs):
    x = np.asarray(inputs["L0_attn_x"], dtype=np.float32)
    shared = _host_layouts(inputs)
    nc = build()
    in_maps = []
    for bi in range(8):
        m = dict(shared)
        m["xT"] = np.ascontiguousarray(x[bi].T)
        in_maps.append(m)
    res = run_bass_kernel_spmd(nc, in_maps, core_ids=list(range(8)))
    out = np.stack([np.ascontiguousarray(res.results[bi]["out"]).T
                    for bi in range(8)])
    return out.astype(np.float32)
